# version 04
# baseline (speedup 1.0000x reference)
import numpy as np
from contextlib import ExitStack

import concourse.bass as bass
import concourse.mybir as mybir
from concourse.bass_utils import run_bass_kernel_spmd

F32 = mybir.dt.float32
BF16 = mybir.dt.bfloat16
AF = mybir.ActivationFunctionType
ALU = mybir.AluOpType
AX = mybir.AxisListType.X

NB, T, D = 2, 2048, 1024
NTOK = NB * T
EPS1, EPSF = 1e-5, 1e-12

ENGS = ["pe", "dve", "act", "pool", "qsp", "qact", "qdve", "qpool"]


class Buf:
    __slots__ = ("w", "r")

    def __init__(self, fence=None):
        self.w = None
        self.r = list(fence) if fence else []


class Prog:
    def __init__(self):
        self.ops = []
        self.lastop = {e: None for e in ENGS}

    def op(self, eng, emit, reads=(), sreads=(), writes=(), dma=False):
        tok = (eng, len(self.ops))

        def edge(p):
            return p[0] != eng or dma or self.ops[p[1]][3]

        deps = set()
        sdeps = set()

        def dep(p):
            if edge(p):
                deps.add(p)
            else:
                sdeps.add(p[1])

        for b in reads:
            if b.w is not None:
                dep(b.w)
            b.r.append(tok)
        for b in sreads:
            if b.w is not None:
                deps.add(b.w)
            b.r.append(tok)
        for b in writes:
            if b.w is not None:
                dep(b.w)
            for t in b.r:
                if t != tok:
                    dep(t)
            b.w = tok
            b.r = []
        deps.discard(tok)
        self.ops.append((eng, emit, deps, dma, sdeps))
        self.lastop[eng] = tok
        return tok

    def fence(self):
        return [t for t in self.lastop.values() if t is not None]

    def emit(self, nc, ctx):
        import os
        trunc = int(os.environ.get("BASS_TRUNC", "0"))
        if trunc > 0:
            self.ops = self.ops[:trunc]
        n = len(self.ops)
        need = [False] * n
        for _eng, _e, deps, _d, _s in self.ops:
            for (_pe, pi) in deps:
                need[pi] = True
        for i, (_eng, _e, _d, is_dma, _s) in enumerate(self.ops):
            if is_dma:
                need[i] = True
        stream = {e: [] for e in ENGS}
        incval = [0] * n
        cnt = {e: 0 for e in ENGS}
        for i, (eng, _e, _d, is_dma, _s) in enumerate(self.ops):
            if need[i]:
                cnt[eng] += 16 if is_dma else 1
            incval[i] = cnt[eng]
            stream[eng].append(i)
        sems = {e: ctx.enter_context(nc.semaphore("sem_" + e)) for e in ENGS}
        ops = self.ops

        import os
        LOG = os.environ.get("BASS_LOG")

        def body(body_engs):
            def run(eng_obj):
                idxs = sorted(i for e in body_engs for i in stream[e])
                water = {}
                drained = {}
                for i in idxs:
                    eng, emit, deps, is_dma, sdeps = ops[i]
                    byeng = {}
                    for (pe_, pi) in deps:
                        v = incval[pi]
                        if byeng.get(pe_, 0) < v:
                            byeng[pe_] = v
                    waits = []
                    pe_waited = False
                    for pe_, v in sorted(byeng.items()):
                        if water.get(pe_, 0) >= v:
                            continue
                        eng_obj.wait_ge(sems[pe_], v)
                        water[pe_] = v
                        waits.append((pe_, v))
                        if pe_ == "pe":
                            pe_waited = True
                    if pe_waited and eng != "pe":
                        eng_obj.nop(cycle_cnt=384)
                    if any(not need[p] and p > drained.get(eng, -1)
                           for p in sdeps) and not is_dma:
                        eng_obj.drain(fusable=True)
                        drained[eng] = i
                    instr = emit(eng_obj)
                    if need[i]:
                        if is_dma:
                            instr.then_inc(sems[eng], 16)
                        else:
                            eng_obj.drain(fusable=True).then_inc(sems[eng], 1)
                            drained[eng] = i
                    if LOG and i < int(LOG):
                        print(f"op{i:5d} {eng:5s} waits={waits} "
                              f"inc={'+16' if is_dma and need[i] else ('+1' if need[i] else '')} v={incval[i]}")
            return run

        with nc.Block() as blk:
            blk.tensor(body(["pe"]))
            blk.vector(body(["dve", "qdve"]))
            blk.scalar(body(["act", "qact"]))
            blk.gpsimd(body(["pool", "qpool"]))
            blk.sync(body(["qsp"]))


class Pool:
    def __init__(self, t, n):
        self.t, self.n, self.off = t, n, 0

    def take(self, nelem, parts=128):
        assert self.off + nelem <= self.n, (
            f"sbuf pool overflow: {self.off}+{nelem} > {self.n}")
        v = self.t[0:parts, self.off:self.off + nelem]
        self.off += nelem
        return v

    def reset(self):
        self.off = 0

    def mark(self):
        return self.off

    def restore(self, m):
        self.off = m


def build():
    nc = bass.Bass("TRN2", target_bir_lowering=False)

    x_in = nc.dram_tensor("x", [NB, T, D], F32, kind="ExternalInput")
    out_t = nc.dram_tensor("out", [NB, T, D], F32, kind="ExternalOutput")

    di = lambda nm, sh: nc.dram_tensor(nm, sh, F32, kind="ExternalInput")
    w_in = di("w_in", [8, 32, 128, 128])
    w_xp = di("w_xp", [16, 128, 96])
    w_dt = di("w_dt", [16, 64, 128])
    dt_bi = di("dt_b", [128, 16])
    a_log = di("a_log", [16, 128, 16])
    dp_i = di("dp", [128, 16])
    convw = di("convw", [16, 128, 4])
    convb = di("convb", [128, 16])
    w_out = di("w_out", [16, 8, 128, 128])
    w_gate = di("w_gate", [8, 8, 128, 128])
    gate_bi = di("gate_b", [128, 8])
    w_fd1 = di("w_fd1", [8, 8, 128, 128])
    fd_b1i = di("fd_b1", [128, 8])
    fd_gi = di("fd_g", [128, 8])
    fd_bbi = di("fd_bb", [128, 8])
    w_fd2 = di("w_fd2", [8, 8, 128, 128])
    fd_b2i = di("fd_b2", [128, 8])
    nrm_gi = di("nrm_g", [128, 8])
    nrm_bi = di("nrm_b", [128, 8])
    bal = di("bal", [1, 1])
    w_r = di("w_r", [8, 128, 4])
    w1_i = di("w1", [32, 4, 8, 128, 128])
    b1_i = di("b1", [128, 128])
    wo_i = di("wo", [32, 8, 128, 128])
    bo_i = di("bo", [128, 8])
    ffn_gi = di("ffn_g", [128, 8])
    ffn_bi = di("ffn_b", [128, 8])
    kst_i = di("kst", [128, 8])
    onesr_i = di("onesr_i", [1, 128])
    iden = di("iden", [128, 128])
    idenb = nc.dram_tensor("idenb", [128, 128], BF16, kind="ExternalInput")
    onesb = nc.dram_tensor("onesb", [128, 1], BF16, kind="ExternalInput")

    import os
    DBG = bool(os.environ.get("BASS_DBG"))
    kw = {"kind": "ExternalOutput"} if DBG else {}
    xc_h = nc.dram_tensor("xc_h", [NB, 16, 128, T], F32, **kw)
    wz_h = nc.dram_tensor("wz_h", [NB, 16, 128, T], BF16, **kw)
    int_h = nc.dram_tensor("int_h", [32, 128, NTOK], BF16, **kw)
    ys_h = nc.dram_tensor("ys_h", [NB, 16, 128, T], F32, **kw)
    m_h = nc.dram_tensor("m_h", [8, 128, NTOK], F32, **kw)
    gt_h = nc.dram_tensor("gt_h", [8, 128, NTOK], F32, **kw)
    dt_h = nc.dram_tensor("dt_h", [8, 128, NTOK], F32, **kw)
    g_h = nc.dram_tensor("g_h", [8, 128, NTOK], F32, **kw)
    if DBG:
        xnb_d = nc.dram_tensor("xnb_d", [128, 8 * NTOK], BF16, **kw)
        lg_d = nc.dram_tensor("lg_d", [4, NTOK], F32, **kw)
        stt_d = nc.dram_tensor("stt_d", [128, 64], F32, **kw)
        yf_d = nc.dram_tensor("yf_d", [128, 4096], F32, **kw)
        prm_d = nc.dram_tensor("prm_d", [128, 48], F32, **kw)
        gn_d = nc.dram_tensor("gn_d", [128, 4096], F32, **kw)
        st1_d = nc.dram_tensor("st1_d", [128, 4096], F32, **kw)
        st2_d = nc.dram_tensor("st2_d", [128, 4096], F32, **kw)
        rep_d = nc.dram_tensor("rep_d", [128, 1024], F32, **kw)
        srow_d = nc.dram_tensor("srow_d", [1, 1536], F32, **kw)
        xnt_d = nc.dram_tensor("xnt_d", [128, 4096], F32, **kw)
        xt_d = nc.dram_tensor("xt_d", [128, 4096], F32, **kw)
    bc_d = nc.dram_tensor("bc_d", [32, T], BF16)
    mask_d = nc.dram_tensor("mask_d", [4, NTOK], BF16, **kw)
    row_d = nc.dram_tensor("row_d", [1, 1024], F32)
    f_d = nc.dram_tensor("f_d", [1, 2], F32)

    ctx = ExitStack()
    sbt = lambda n_, s_, d_: ctx.enter_context(nc.sbuf_tensor(n_, s_, d_))

    xnb = sbt("xnb", [128, 8 * NTOK], BF16)
    mask_fm = sbt("mask_fm", [4, NTOK], BF16)
    id_f = sbt("id_f", [128, 128], F32)
    id_b = sbt("id_b", [128, 128], BF16)
    ones_sb = sbt("ones_sb", [128, 1], BF16)
    a_sb = sbt("a_sb", [128, 256], F32)
    dp_sb = sbt("dp_sb", [128, 16], F32)
    cw_sb = sbt("cw_sb", [128, 64], F32)
    cb_sb = sbt("cb_sb", [128, 16], F32)
    dtb_sb = sbt("dtb_sb", [128, 16], F32)
    wr_sb = sbt("wr_sb", [128, 32], F32)
    wxp_b = sbt("wxp_b", [128, 16 * 96], BF16)
    wdt_b = sbt("wdt_b", [64, 16 * 128], BF16)
    gateb_sb = sbt("gateb_sb", [128, 8], F32)
    fdb1_sb = sbt("fdb1_sb", [128, 8], F32)
    fdg_sb = sbt("fdg_sb", [128, 8], F32)
    fdbb_sb = sbt("fdbb_sb", [128, 8], F32)
    fdb2_sb = sbt("fdb2_sb", [128, 8], F32)
    nrmg_sb = sbt("nrmg_sb", [128, 8], F32)
    nrmb_sb = sbt("nrmb_sb", [128, 8], F32)
    b1_sb = sbt("b1_sb", [128, 128], F32)
    bo_sb = sbt("bo_sb", [128, 8], F32)
    ffng_sb = sbt("ffng_sb", [128, 8], F32)
    ffnb_sb = sbt("ffnb_sb", [128, 8], F32)
    kst = sbt("kst_s", [128, 8], F32)
    onesr = sbt("onesr", [1, 128], F32)
    f_sb = sbt("f_sb", [128, 2], F32)
    bal_sb = sbt("bal_sb", [1, 2], F32)
    hlast = sbt("hlast", [128, 256], F32)
    stt = sbt("stt", [128, 64], F32)

    PFN = 13312
    PBN = 35840
    pf_t = sbt("pf", [128, PFN], F32)
    pb_t = sbt("pb", [128, PBN], BF16)
    ps_t = ctx.enter_context(nc.psum_tensor("ps", [128, 4096], F32))
    pf = Pool(pf_t, PFN)
    pb = Pool(pb_t, PBN)

    def bank(k, parts=128, width=512, off=0):
        return ps_t[0:parts, k * 512 + off:k * 512 + off + width]

    P = Prog()
    psb = [Buf() for _ in range(8)]

    def nb(k=1, fence=None):
        return [Buf(fence) for _ in range(k)] if k > 1 else Buf(fence)

    wb = {}
    for name, dst, src in [
        ("idf", id_f[:, :], iden[:, :]),
        ("idb", id_b[:, :], idenb[:, :]),
        ("ones", ones_sb[:, :], onesb[:, :]),
        ("dp", dp_sb[:, :], dp_i[:, :]),
        ("cb", cb_sb[:, :], convb[:, :]),
        ("dtbias", dtb_sb[:, :], dt_bi[:, :]),
        ("gateb", gateb_sb[:, :], gate_bi[:, :]),
        ("fdb1", fdb1_sb[:, :], fd_b1i[:, :]),
        ("fdg", fdg_sb[:, :], fd_gi[:, :]),
        ("fdbb", fdbb_sb[:, :], fd_bbi[:, :]),
        ("fdb2", fdb2_sb[:, :], fd_b2i[:, :]),
        ("nrmg", nrmg_sb[:, :], nrm_gi[:, :]),
        ("nrmb", nrmb_sb[:, :], nrm_bi[:, :]),
        ("bo", bo_sb[:, :], bo_i[:, :]),
        ("ffng", ffng_sb[:, :], ffn_gi[:, :]),
        ("ffnb", ffnb_sb[:, :], ffn_bi[:, :]),
        ("a", a_sb[:, :].rearrange("p (c s) -> p c s", c=16),
         a_log[:, :, :].rearrange("c p s -> p c s")),
        ("cw", cw_sb[:, :].rearrange("p (c j) -> p c j", c=16),
         convw[:, :, :].rearrange("c p j -> p c j")),
        ("wr", wr_sb[:, :].rearrange("p (c e) -> p c e", c=8),
         w_r[:, :, :].rearrange("c p e -> p c e")),
        ("b1", b1_sb[:, :], b1_i[:, :]),
        ("bal", bal_sb[0:1, 0:1], bal[:, :]),
        ("kst", kst[:, :], kst_i[:, :]),
        ("onesr", onesr[:, :], onesr_i[:, :]),
    ]:
        wb[name] = Buf()
        P.op("qsp", (lambda d_, s_: lambda e: e.dma_start(d_, s_))(dst, src),
             writes=[wb[name]], dma=True)

    wxpf = pf.take(16 * 96)
    wdtf = pf.take(16 * 128)
    wb["wxpf"], wb["wxpb"], wb["wdtf"], wb["wdtb"] = (Buf(), Buf(), Buf(),
                                                      Buf())
    P.op("qsp", lambda e: e.dma_start(
        wxpf.rearrange("p (c m) -> p c m", c=16),
        w_xp[:, :, :].rearrange("c p m -> p c m")),
        writes=[wb["wxpf"]], dma=True)
    P.op("qsp", lambda e: e.dma_start(
        wdtf[0:64, :].rearrange("p (c m) -> p c m", c=16),
        w_dt[:, :, :].rearrange("c p m -> p c m")),
        writes=[wb["wdtf"]], dma=True)
    P.op("dve", lambda e: e.tensor_copy(wxp_b[:, :], wxpf),
         reads=[wb["wxpf"]], writes=[wb["wxpb"]])
    P.op("dve", lambda e: e.tensor_copy(wdt_b[:, :], wdtf[0:64, :]),
         reads=[wb["wdtf"]], writes=[wb["wdtb"]])
    P.op("act", lambda e: e.activation(a_sb[:, :], a_sb[:, :], AF.Exp),
         reads=[wb["a"]], writes=[wb["a"]])
    P.op("dve", lambda e: e.tensor_scalar_mul(a_sb[:, :], a_sb[:, :], -1.0),
         reads=[wb["a"]], writes=[wb["a"]])
    P.op("act", lambda e: e.activation(bal_sb[0:1, 0:1], bal_sb[0:1, 0:1],
                                       AF.Sigmoid),
         reads=[wb["bal"]], writes=[wb["bal"]])
    P.op("dve", lambda e: e.tensor_scalar(bal_sb[0:1, 1:2], bal_sb[0:1, 0:1],
                                          -1.0, 1.0, op0=ALU.mult,
                                          op1=ALU.add),
         reads=[wb["bal"]], writes=[wb["bal"]])
    wb["f"] = Buf()
    wb["fd_"] = Buf()
    P.op("qsp", lambda e: e.dma_start(f_d[:, :], bal_sb[0:1, :]),
         reads=[wb["bal"]], writes=[wb["fd_"]], dma=True)
    P.op("qsp", lambda e: e.dma_start(
        f_sb[:, :], f_d[0:1, :].broadcast_to([128, 2])),
        reads=[wb["fd_"]], writes=[wb["f"]], dma=True)

    bxnb = Buf()
    bmask = Buf()
    bhl = [Buf() for _ in range(256)]

    pf.reset()
    fence = P.fence()
    xt = pf.take(4 * 1024)
    xnt = pf.take(4 * 1024)
    xnf = pf.take(2 * 512)
    logits = pf.take(4096, parts=4)
    bxt = nb(4, fence)
    bxnt = nb(4, fence)
    bxnf = nb(2, fence)
    bst = nb(1, fence)
    blog = nb(1, fence)

    for it in range(NB * 4):
        b, tc = divmod(it, 4)
        for tt in range(4):
            P.op("qsp", (lambda b_, tc_, tt_: lambda e: e.dma_start(
                xt[:, tt_ * 1024:(tt_ + 1) * 1024],
                x_in[b_, tc_ * 512 + tt_ * 128:tc_ * 512 + (tt_ + 1) * 128, :]))
                (b, tc, tt), writes=[bxt[tt]], dma=True)
        for tt in range(4):
            P.op("dve", (lambda tt_: lambda e: e.reduce_sum(
                stt[:, tt_:tt_ + 1], xt[:, tt_ * 1024:(tt_ + 1) * 1024],
                axis=AX))(tt), reads=[bxt[tt]], writes=[bst])
            P.op("act", (lambda tt_: lambda e: e.activation(
                xnf[:, 0:1024], xt[:, tt_ * 1024:(tt_ + 1) * 1024],
                AF.Square))(tt),
                reads=[bxt[tt]], writes=[bxnf[0], bxnf[1]])
            P.op("dve", (lambda tt_: lambda e: e.reduce_sum(
                stt[:, 4 + tt_:5 + tt_], xnf[:, 0:1024], axis=AX))(tt),
                reads=[bxnf[0], bxnf[1]], writes=[bst])

        def stats(e):
            e.tensor_scalar_mul(stt[:, 0:4], stt[:, 0:4], 1.0 / D)
            e.tensor_scalar_mul(stt[:, 4:8], stt[:, 4:8], 1.0 / D)
            e.drain(fusable=True)
            e.tensor_tensor(stt[:, 8:12], stt[:, 0:4], stt[:, 0:4],
                            op=ALU.mult)
            e.drain(fusable=True)
            return e.tensor_tensor(stt[:, 4:8], stt[:, 4:8], stt[:, 8:12],
                                   op=ALU.subtract)
        P.op("dve", stats, reads=[bst], writes=[bst])

        def rcalc(e):
            e.activation(stt[:, 8:12], stt[:, 4:8], AF.Ln, bias=kst[:, 0:1])
            e.drain(fusable=True)
            return e.activation(stt[:, 8:12], stt[:, 8:12], AF.Exp,
                                scale=kst[:, 2:3])
        P.op("act", rcalc, reads=[bst], sreads=[wb["kst"]], writes=[bst])

        def bcalc(e):
            e.tensor_tensor(stt[:, 12:16], stt[:, 0:4], stt[:, 8:12],
                            op=ALU.mult)
            e.drain(fusable=True)
            return e.tensor_scalar_mul(stt[:, 12:16], stt[:, 12:16], -1.0)
        P.op("dve", bcalc, reads=[bst], writes=[bst])
        for tt in range(4):
            P.op("dve", (lambda tt_: lambda e: e.tensor_scalar(
                xnt[:, tt_ * 1024:(tt_ + 1) * 1024],
                xt[:, tt_ * 1024:(tt_ + 1) * 1024],
                stt[:, 8 + tt_:9 + tt_], stt[:, 12 + tt_:13 + tt_],
                op0=ALU.mult, op1=ALU.add))(tt),
                reads=[bxt[tt]], sreads=[bst], writes=[bxnt[tt]])
        if DBG and it == 0:
            P.op("qsp", lambda e: e.dma_start(stt_d[:, :], stt[:, :]),
                 reads=[bst], dma=True)
            P.op("qsp", lambda e: e.dma_start(xnt_d[:, :], xnt[:, :]),
                 reads=list(bxnt), dma=True)
            P.op("qsp", lambda e: e.dma_start(xt_d[:, :], xt[:, :]),
                 reads=list(bxt), dma=True)
        for dc in range(8):
            pk = dc % 2

            def tr(dc_, pk_):
                def f(e):
                    r = None
                    for tt_ in range(4):
                        r = e.transpose(
                            bank(pk_)[:, tt_ * 128:(tt_ + 1) * 128],
                            xnt[:, tt_ * 1024 + dc_ * 128:
                                tt_ * 1024 + (dc_ + 1) * 128],
                            id_f[:, :])
                    return r
                return f
            P.op("pe", tr(dc, pk), reads=bxnt + [wb["idf"]],
                 writes=[psb[pk]])
            col = dc * NTOK + b * T + tc * 512
            P.op("dve", (lambda pk_, col_: lambda e: e.tensor_copy(
                xnb[:, col_:col_ + 512], bank(pk_)))(pk, col),
                reads=[psb[pk]], writes=[bxnb])
            P.op("dve", (lambda pk_: lambda e: e.tensor_copy(
                xnf[:, pk_ * 512:(pk_ + 1) * 512], bank(pk_)))(pk),
                reads=[psb[pk]], writes=[bxnf[pk]])
            P.op("pe", (lambda dc_, pk_: lambda e: e.matmul(
                bank(2, parts=4), wr_sb[:, dc_ * 4:(dc_ + 1) * 4],
                xnf[:, pk_ * 512:(pk_ + 1) * 512],
                start=(dc_ == 0), stop=(dc_ == 7)))(dc, pk),
                reads=[bxnf[pk], wb["wr"]], writes=[psb[2]])
        P.op("dve", (lambda c0: lambda e: e.tensor_copy(
            logits[:, c0:c0 + 512], bank(2, parts=4)))(b * T + tc * 512),
            reads=[psb[2]], writes=[blog])

    bon = nb(2, fence)
    for ch in range(32):
        pk = ch % 2
        c0 = 32 + pk * 8
        P.op("pe", (lambda ch_, pk_: lambda e: e.transpose(
            bank(4 + pk_, width=4), logits[:, ch_ * 128:(ch_ + 1) * 128],
            id_f[0:4, 0:4]))(ch, pk),
            reads=[blog, wb["idf"]], writes=[psb[4 + pk]])
        P.op("dve", (lambda pk_, c_: lambda e: e.tensor_copy(
            stt[:, c_:c_ + 4], bank(4 + pk_, width=4)))(pk, c0),
            reads=[psb[4 + pk]], writes=[bon[pk]])
        P.op("dve", (lambda pk_, c_: lambda e: e.reduce_max(
            stt[:, c_ + 4:c_ + 5], stt[:, c_:c_ + 4], axis=AX))(pk, c0),
            reads=[bon[pk]], writes=[bon[pk]])
        P.op("dve", (lambda pk_, c_: lambda e: e.tensor_single_scalar(
            stt[:, c_:c_ + 4], stt[:, c_:c_ + 4], stt[:, c_ + 4:c_ + 5],
            op=ALU.is_ge))(pk, c0),
            reads=[bon[pk]], sreads=[bon[pk]], writes=[bon[pk]])
        P.op("pe", (lambda pk_, c_: lambda e: e.transpose(
            bank(6 + pk_, parts=4, width=128), stt[:, c_:c_ + 4],
            id_f[:, :]))(pk, c0),
            reads=[bon[pk], wb["idf"]], writes=[psb[6 + pk]])
        P.op("dve", (lambda ch_, pk_: lambda e: e.tensor_copy(
            mask_fm[:, ch_ * 128:(ch_ + 1) * 128],
            bank(6 + pk_, parts=4, width=128)))(ch, pk),
            reads=[psb[6 + pk]], writes=[bmask])

    if DBG:
        P.op("qsp", lambda e: e.dma_start(xnb_d[:, :], xnb[:, :]),
             reads=[bxnb], dma=True)
        P.op("qsp", lambda e: e.dma_start(lg_d[:, :], logits[:, :]),
             reads=[blog], dma=True)

    pf.reset()
    pb.reset()
    fence = P.fence()
    w1f = pf.take(2 * 4096)
    w1b = pb.take(2 * 4096)
    mrep = pb.take(4 * 4096)
    acc = pb.take(2 * 512)
    gtmp = pb.take(2 * 512)
    qm = pb.take(2 * 512)
    bw1f = nb(2, fence)
    bw1b = nb(2, fence)
    bmrep = nb(4, fence)
    bacc = nb(2, fence)
    bgt = nb(2, fence)
    bqm = nb(2, fence)
    bmaskd = nb(1, fence)
    P.op("qsp", lambda e: e.dma_start(mask_d[:, :], mask_fm[:, :]),
         reads=[bmask], writes=[bmaskd], dma=True)
    for ee in range(4):
        P.op("qsp", (lambda ee_: lambda e: e.dma_start(
            mrep[:, ee_ * 4096:(ee_ + 1) * 4096],
            mask_d[ee_:ee_ + 1, :].broadcast_to([128, NTOK])))(ee),
            reads=[bmaskd], writes=[bmrep[ee]], dma=True)
    for mc in range(32):
        sl = mc % 2
        P.op("qsp", (lambda mc_, sl_: lambda e: e.dma_start(
            w1f[:, sl_ * 4096:(sl_ + 1) * 4096].rearrange(
                "p (ee k f) -> p ee k f", ee=4, k=8),
            w1_i[mc_].rearrange("ee k p f -> p ee k f")))(mc, sl),
            writes=[bw1f[sl]], dma=True)
        P.op("dve", (lambda sl_: lambda e: e.tensor_copy(
            w1b[:, sl_ * 4096:(sl_ + 1) * 4096],
            w1f[:, sl_ * 4096:(sl_ + 1) * 4096]))(sl),
            reads=[bw1f[sl]], writes=[bw1b[sl]])
        for ts in range(8):
            asl = (mc * 8 + ts) % 2
            pa = 2 + asl
            for ei, ee in enumerate([3, 2, 1, 0]):
                pk = ei % 2
                gsl = ei % 2
                qsl = ei % 2

                def mm1(sl_, ee_, ts_, pk_):
                    def f(e):
                        r = None
                        for k in range(8):
                            r = e.matmul(
                                bank(pk_),
                                w1b[:, sl_ * 4096 + (ee_ * 8 + k) * 128:
                                    sl_ * 4096 + (ee_ * 8 + k + 1) * 128],
                                xnb[:, k * NTOK + ts_ * 512:
                                    k * NTOK + (ts_ + 1) * 512],
                                start=(k == 0), stop=(k == 7))
                        return r
                    return f
                P.op("pe", mm1(sl, ee, ts, pk),
                     reads=[bw1b[sl], bxnb], writes=[psb[pk]])
                P.op("act", (lambda pk_, gsl_, ee_, mc_: lambda e:
                     e.activation(
                         gtmp[:, gsl_ * 512:(gsl_ + 1) * 512], bank(pk_),
                         AF.Gelu,
                         bias=b1_sb[:, ee_ * 32 + mc_:ee_ * 32 + mc_ + 1]))
                     (pk, gsl, ee, mc),
                     reads=[psb[pk]], sreads=[wb["b1"]], writes=[bgt[gsl]])
                P.op("dve", (lambda gsl_, qsl_, ee_, ts_: lambda e:
                     e.tensor_tensor(
                         qm[:, qsl_ * 512:(qsl_ + 1) * 512],
                         gtmp[:, gsl_ * 512:(gsl_ + 1) * 512],
                         mrep[:, ee_ * 4096 + ts_ * 512:
                              ee_ * 4096 + (ts_ + 1) * 512],
                         op=ALU.mult))(gsl, qsl, ee, ts),
                     reads=[bgt[gsl], bmrep[ee]], writes=[bqm[qsl]])
                P.op("pe", (lambda qsl_, ei_, pa_: lambda e: e.matmul(
                    bank(pa_), id_b[:, :],
                    qm[:, qsl_ * 512:(qsl_ + 1) * 512],
                    start=(ei_ == 0), stop=(ei_ == 3)))(qsl, ei, pa),
                    reads=[bqm[qsl], wb["idb"]], writes=[psb[pa]])
            P.op("act", (lambda pa_, asl_: lambda e: e.activation(
                acc[:, asl_ * 512:(asl_ + 1) * 512], bank(pa_), AF.Copy))
                (pa, asl),
                reads=[psb[pa]], writes=[bacc[asl]])
            P.op("qpool", (lambda mc_, ts_, asl_: lambda e: e.dma_start(
                int_h[mc_, :, ts_ * 512:(ts_ + 1) * 512],
                acc[:, asl_ * 512:(asl_ + 1) * 512]))(mc, ts, asl),
                reads=[bacc[asl]], dma=True)

    pf.reset()
    pb.reset()
    fence = P.fence()
    srow = pf.take(1536)
    wof = pf.take(2 * 2048)
    yf = pf.take(8 * 512)
    reps = pf.take(2 * 512)
    nsc = pf.take(512)
    wob = pb.take(32 * 8 * 128)
    itb = pb.take(2 * 512)
    yfb = pb.take(512)
    sqb = pb.take(512)
    bwof = nb(2, fence)
    bwob = nb(1, fence)
    bitb = nb(2, fence)
    byf = nb(8, fence)
    byfb = nb(1, fence)
    bsqb = nb(1, fence)
    breps = nb(1, fence)
    brow = nb(1, fence)
    browd_f2 = nb(1, fence)
    for kk in range(16):
        sl = kk % 2
        P.op("qsp", (lambda kk_, sl_: lambda e: e.dma_start(
            wof[:, sl_ * 2048:(sl_ + 1) * 2048].rearrange(
                "p (k m f) -> p k m f", k=2, m=8),
            wo_i[kk_ * 2:(kk_ + 1) * 2].rearrange("k m p f -> p k m f")))
            (kk, sl), writes=[bwof[sl]], dma=True)
        P.op("dve", (lambda kk_, sl_: lambda e: e.tensor_copy(
            wob[:, kk_ * 2048:(kk_ + 1) * 2048],
            wof[:, sl_ * 2048:(sl_ + 1) * 2048]))(kk, sl),
            reads=[bwof[sl]], writes=[bwob])
    gout = wof
    bgout = nb(8, P.fence())
    for tc in range(8):
        for k in range(32):
            sl = k % 2
            P.op("qsp", (lambda k_, tc_, sl_: lambda e: e.dma_start(
                itb[:, sl_ * 512:(sl_ + 1) * 512],
                int_h[k_, :, tc_ * 512:(tc_ + 1) * 512]))(k, tc, sl),
                writes=[bitb[sl]], dma=True)
            for dm in range(8):
                P.op("pe", (lambda k_, dm_, sl_: lambda e: e.matmul(
                    bank(dm_), wob[:, (k_ * 8 + dm_) * 128:
                                   (k_ * 8 + dm_ + 1) * 128],
                    itb[:, sl_ * 512:(sl_ + 1) * 512],
                    start=(k_ == 0), stop=(k_ == 31)))(k, dm, sl),
                    reads=[bwob, bitb[sl]], writes=[psb[dm]])
        for dm in range(8):
            P.op("dve", (lambda dm_: lambda e: e.tensor_scalar(
                yf[:, dm_ * 512:(dm_ + 1) * 512], bank(dm_),
                bo_sb[:, dm_:dm_ + 1], None, op0=ALU.add))(dm),
                reads=[psb[dm]], sreads=[wb["bo"]], writes=[byf[dm]])
            P.op("dve", (lambda dm_, tc_: lambda e: e.tensor_tensor(
                yf[:, dm_ * 512:(dm_ + 1) * 512],
                yf[:, dm_ * 512:(dm_ + 1) * 512],
                xnb[:, dm_ * NTOK + tc_ * 512:dm_ * NTOK + (tc_ + 1) * 512],
                op=ALU.add))(dm, tc),
                reads=[byf[dm], bxnb], writes=[byf[dm]])
        for dm in range(8):
            P.op("dve", (lambda dm_: lambda e: e.tensor_copy(
                yfb[:, :], yf[:, dm_ * 512:(dm_ + 1) * 512]))(dm),
                reads=[byf[dm]], writes=[byfb])
            P.op("act", (lambda dm_: lambda e: e.activation(
                sqb[:, :], yf[:, dm_ * 512:(dm_ + 1) * 512], AF.Square))(dm),
                reads=[byf[dm]], writes=[bsqb])
            P.op("pe", (lambda dm_: lambda e: e.matmul(
                bank(0, parts=1), ones_sb[:, :], yfb[:, :],
                start=(dm_ == 0), stop=(dm_ == 7)))(dm),
                reads=[byfb, wb["ones"]], writes=[psb[0]])
            P.op("pe", (lambda dm_: lambda e: e.matmul(
                bank(1, parts=1), ones_sb[:, :], sqb[:, :],
                start=(dm_ == 0), stop=(dm_ == 7)))(dm),
                reads=[bsqb, wb["ones"]], writes=[psb[1]])

        if DBG and tc == 0:
            P.op("qsp", lambda e: e.dma_start(yf_d[:, :], yf[:, :]),
                 reads=list(byf), dma=True)

        def rowmath(e):
            e.tensor_copy(srow[0:1, 0:512], bank(0, parts=1))
            e.tensor_copy(srow[0:1, 512:1024], bank(1, parts=1))
            e.tensor_scalar_mul(srow[0:1, 0:512], srow[0:1, 0:512], 1.0 / D)
            e.tensor_scalar_mul(srow[0:1, 512:1024], srow[0:1, 512:1024],
                                1.0 / D)
            e.drain(fusable=True)
            e.tensor_tensor(srow[0:1, 1024:1536], srow[0:1, 0:512],
                            srow[0:1, 0:512], op=ALU.mult)
            e.drain(fusable=True)
            return e.tensor_tensor(srow[0:1, 512:1024], srow[0:1, 512:1024],
                                   srow[0:1, 1024:1536], op=ALU.subtract)
        P.op("dve", rowmath, reads=[psb[0], psb[1]], writes=[brow])

        def rrow(e):
            e.activation(srow[0:1, 512:1024], srow[0:1, 512:1024], AF.Ln,
                         bias=kst[0:1, 1:2])
            e.drain(fusable=True)
            return e.activation(srow[0:1, 512:1024], srow[0:1, 512:1024],
                                AF.Exp, scale=kst[0:1, 2:3])
        P.op("act", rrow, reads=[brow], sreads=[wb["kst"]], writes=[brow])
        if DBG and tc == 0:
            P.op("qsp", lambda e: e.dma_start(srow_d[:, :], srow[0:1, :]),
                 reads=[brow], dma=True)
        P.op("pe", lambda e: e.matmul(
            bank(0), onesr[:, :], srow[0:1, 0:512], start=True, stop=True),
            reads=[brow, wb["onesr"]], writes=[psb[0]])
        P.op("pe", lambda e: e.matmul(
            bank(1), onesr[:, :], srow[0:1, 512:1024], start=True,
            stop=True), reads=[brow, wb["onesr"]], writes=[psb[1]])
        P.op("dve", lambda e: e.tensor_copy(reps[:, 0:512], bank(0)),
             reads=[psb[0]], writes=[breps])
        P.op("dve", lambda e: e.tensor_copy(reps[:, 512:1024], bank(1)),
             reads=[psb[1]], writes=[breps])
        for dm in range(8):
            def norm1(dm_):
                def f(e):
                    e.tensor_tensor(nsc[:, 0:512],
                                    yf[:, dm_ * 512:(dm_ + 1) * 512],
                                    reps[:, 0:512], op=ALU.subtract)
                    e.drain(fusable=True)
                    e.tensor_tensor(
                        gout[:, dm_ * 512:(dm_ + 1) * 512],
                        nsc[:, 0:512],
                        reps[:, 512:1024], op=ALU.mult)
                    e.drain(fusable=True)
                    return e.tensor_scalar(
                        gout[:, dm_ * 512:(dm_ + 1) * 512],
                        gout[:, dm_ * 512:(dm_ + 1) * 512],
                        ffng_sb[:, dm_:dm_ + 1], ffnb_sb[:, dm_:dm_ + 1],
                        op0=ALU.mult, op1=ALU.add)
                return f
            P.op("dve", norm1(dm), reads=[byf[dm], breps],
                 sreads=[wb["ffng"], wb["ffnb"]], writes=[bgout[dm]])
            if DBG and tc == 1:
                P.op("qsp", (lambda dm_: lambda e: e.dma_start(
                    st1_d[:, dm_ * 512:(dm_ + 1) * 512],
                    gout[:, dm_ * 512:(dm_ + 1) * 512]))(dm),
                    reads=[bgout[dm]], dma=True)
            P.op("act", (lambda dm_: lambda e: e.activation(
                gout[:, dm_ * 512:(dm_ + 1) * 512],
                gout[:, dm_ * 512:(dm_ + 1) * 512], AF.Silu))(dm),
                reads=[bgout[dm]], writes=[bgout[dm]])
            P.op("qpool", (lambda dm_, tc_: lambda e: e.dma_start(
                g_h[dm_, :, tc_ * 512:(tc_ + 1) * 512],
                gout[:, dm_ * 512:(dm_ + 1) * 512]))(dm, tc),
                reads=[bgout[dm]], dma=True)

    for b in range(NB):
        pf.reset()
        pb.reset()
        fence = P.fence()
        xdblb = pb.take(2048, parts=96)
        pbmark = pb.mark()
        winf = pf.take(2 * 1024)
        xi = pf.take(2054)
        c0t = pf.take(2048)
        xct = pf.take(2 * 2048)
        xdbl = pf.take(2048, parts=96)
        winb = pb.take(2 * 1024)
        xcb = pb.take(2048)
        wzt = pb.take(2 * 2048)
        bwinf = nb(2, fence)
        bwinb = nb(2, fence)
        bxi = nb(1, fence)
        bc0 = nb(1, fence)
        bxc = nb(2, fence)
        bxcb = nb(1, fence)
        bwz = nb(2, fence)
        bxdbl = nb(1, fence)
        bxdblb = nb(1, fence)
        P.op("dve", lambda e: e.memset(xi[:, 0:3], 0.0), writes=[bxi])
        for m in range(32):
            sl = m % 2
            P.op("qsp", (lambda m_, sl_: lambda e: e.dma_start(
                winf[:, sl_ * 1024:(sl_ + 1) * 1024].rearrange(
                    "p (k f) -> p k f", k=8),
                w_in[:, m_].rearrange("k p f -> p k f")))(m, sl),
                writes=[bwinf[sl]], dma=True)
            P.op("dve", (lambda sl_: lambda e: e.tensor_copy(
                winb[:, sl_ * 1024:(sl_ + 1) * 1024],
                winf[:, sl_ * 1024:(sl_ + 1) * 1024]))(sl),
                reads=[bwinf[sl]], writes=[bwinb[sl]])
            for ts in range(4):
                def mmb(sl_, ts_, b_):
                    def f(e):
                        r = None
                        for k in range(8):
                            r = e.matmul(
                                bank(ts_),
                                winb[:, sl_ * 1024 + k * 128:
                                     sl_ * 1024 + (k + 1) * 128],
                                xnb[:, k * NTOK + b_ * T + ts_ * 512:
                                    k * NTOK + b_ * T + (ts_ + 1) * 512],
                                start=(k == 0), stop=(k == 7))
                        return r
                    return f
                P.op("pe", mmb(sl, ts, b),
                     reads=[bwinb[sl], bxnb], writes=[psb[ts]])
            if m < 16:
                cc = m

                def evac_xi(sl_):
                    x0 = 3

                    def f(e):
                        r = None
                        for ts_ in range(4):
                            r = e.copy(
                                xi[:, x0 + ts_ * 512:x0 + (ts_ + 1) * 512],
                                bank(ts_))
                        return r
                    return f
                P.op("act", evac_xi(sl),
                     reads=[psb[0], psb[1], psb[2], psb[3]], writes=[bxi])

                def conv(sl_, cc_):
                    x0 = 0

                    def f(e):
                        e.tensor_scalar_mul(
                            c0t[:, 0:2048], xi[:, x0 + 3:x0 + 2051],
                            cw_sb[:, cc_ * 4 + 3:cc_ * 4 + 4])
                        r = None
                        for j in (2, 1, 0):
                            e.drain(fusable=True)
                            r = e.scalar_tensor_tensor(
                                c0t[:, 0:2048], xi[:, x0 + j:x0 + j + 2048],
                                cw_sb[:, cc_ * 4 + j:cc_ * 4 + j + 1],
                                c0t[:, 0:2048], op0=ALU.mult, op1=ALU.add)
                        return r
                    return f
                P.op("dve", conv(sl, cc), reads=[bxi],
                     sreads=[wb["cw"]], writes=[bc0])
                P.op("act", (lambda sl_, cc_: lambda e: e.activation(
                    xct[:, sl_ * 2048:(sl_ + 1) * 2048], c0t[:, 0:2048],
                    AF.Silu, bias=cb_sb[:, cc_:cc_ + 1]))(sl, cc),
                    reads=[bc0], sreads=[wb["cb"]], writes=[bxc[sl]])
                P.op("qpool", (lambda b_, cc_, sl_: lambda e: e.dma_start(
                    xc_h[b_, cc_], xct[:, sl_ * 2048:(sl_ + 1) * 2048]))
                    (b, cc, sl), reads=[bxc[sl]], dma=True)
                P.op("dve", (lambda sl_: lambda e: e.tensor_copy(
                    xcb[:, :], xct[:, sl_ * 2048:(sl_ + 1) * 2048]))(sl),
                    reads=[bxc[sl]], writes=[bxcb])
                for ts in range(4):
                    P.op("pe", (lambda cc_, ts_: lambda e: e.matmul(
                        bank(4 + ts_, parts=96),
                        wxp_b[:, cc_ * 96:(cc_ + 1) * 96],
                        xcb[:, ts_ * 512:(ts_ + 1) * 512],
                        start=(cc_ == 0), stop=(cc_ == 15)))(cc, ts),
                        reads=[bxcb, wb["wxpb"]], writes=[psb[4 + ts]])
            else:
                cc = m - 16

                def evac_z(sl_):
                    def f(e):
                        r = None
                        for ts_ in range(4):
                            r = e.activation(
                                wzt[:, sl_ * 2048 + ts_ * 512:
                                    sl_ * 2048 + (ts_ + 1) * 512],
                                bank(ts_), AF.Silu)
                        return r
                    return f
                P.op("act", evac_z(sl),
                     reads=[psb[0], psb[1], psb[2], psb[3]], writes=[bwz[sl]])
                P.op("qpool", (lambda b_, cc_, sl_: lambda e: e.dma_start(
                    wz_h[b_, cc_], wzt[:, sl_ * 2048:(sl_ + 1) * 2048]))
                    (b, cc, sl), reads=[bwz[sl]], dma=True)

        def evac_xdbl(e):
            r = None
            for ts_ in range(4):
                r = e.copy(xdbl[:, ts_ * 512:(ts_ + 1) * 512],
                           bank(4 + ts_, parts=96))
            return r
        P.op("act", evac_xdbl, reads=[psb[4], psb[5], psb[6], psb[7]],
             writes=[bxdbl])
        P.op("dve", lambda e: e.tensor_copy(xdblb[:, :], xdbl[:, :]),
             reads=[bxdbl], writes=[bxdblb])
        bbcd = nb(1, fence)
        P.op("qsp", lambda e: e.dma_start(bc_d[:, :], xdblb[64:96, :]),
             reads=[bxdblb], writes=[bbcd], dma=True)

        pf.reset()
        pb.restore(pbmark)
        fence = P.fence()
        delta = pf.take(2 * 1024)
        du = pf.take(2 * 1024)
        dab = pf.take(2 * 1024)
        dbu = pf.take(2 * 1024)
        hb = pf.take(2 * 1024)
        yst = pf.take(1024)
        xcr = pf.take(1024)
        spw = pf.take(512)
        bct = pb.take(32 * 1024)
        qb = pb.take(1024)
        bdelta = nb(2, fence)
        bdu = nb(2, fence)
        bda = nb(2, fence)
        bdbu = nb(2, fence)
        bh = nb(2, fence)
        bys = nb(1, fence)
        bxcr = nb(1, fence)
        bspw = nb(1, fence)
        bbc = nb(32, fence)
        bq = nb(1, fence)
        for th in range(2):
            for s in range(16):
                P.op("qsp", (lambda s_, th_: lambda e: e.dma_start(
                    bct[:, s_ * 1024:(s_ + 1) * 1024],
                    bc_d[s_:s_ + 1, th_ * 1024:(th_ + 1) * 1024]
                    .broadcast_to([128, 1024])))(s, th),
                    reads=[bbcd], writes=[bbc[s]], dma=True)
                P.op("qsp", (lambda s_, th_: lambda e: e.dma_start(
                    bct[:, (16 + s_) * 1024:(17 + s_) * 1024],
                    bc_d[16 + s_:17 + s_, th_ * 1024:(th_ + 1) * 1024]
                    .broadcast_to([128, 1024])))(s, th),
                    reads=[bbcd], writes=[bbc[16 + s]], dma=True)
            for cc in range(16):
                csl = cc % 2
                P.op("qsp", (lambda b_, cc_, th_: lambda e: e.dma_start(
                    xcr[:, 0:1024],
                    xc_h[b_, cc_, :, th_ * 1024:(th_ + 1) * 1024]))
                    (b, cc, th), writes=[bxcr], dma=True)
                for ts in range(2):
                    P.op("pe", (lambda cc_, th_, ts_: lambda e: e.matmul(
                        bank(ts_), wdt_b[:, cc_ * 128:(cc_ + 1) * 128],
                        xdblb[0:64, th_ * 1024 + ts_ * 512:
                              th_ * 1024 + (ts_ + 1) * 512],
                        start=True, stop=True))(cc, th, ts),
                        reads=[bxdblb, wb["wdtb"]], writes=[psb[ts]])

                    def softp(csl_, ts_, cc_):
                        o = csl_ * 1024 + ts_ * 512

                        def f(e):
                            e.activation(delta[:, o:o + 512], bank(ts_),
                                         AF.Relu,
                                         bias=dtb_sb[:, cc_:cc_ + 1])
                            e.activation(spw[:, 0:512], bank(ts_), AF.Abs,
                                         bias=dtb_sb[:, cc_:cc_ + 1])
                            e.drain(fusable=True)
                            e.activation(spw[:, 0:512], spw[:, 0:512],
                                         AF.Exp, scale=kst[:, 3:4])
                            e.drain(fusable=True)
                            return e.activation(spw[:, 0:512], spw[:, 0:512],
                                                AF.Ln, bias=kst[:, 4:5])
                        return f
                    P.op("act", softp(csl, ts, cc), reads=[psb[ts]],
                         sreads=[wb["dtbias"], wb["kst"]],
                         writes=[bdelta[csl], bspw])
                    P.op("dve", (lambda csl_, ts_: lambda e: e.tensor_tensor(
                        delta[:, csl_ * 1024 + ts_ * 512:
                              csl_ * 1024 + (ts_ + 1) * 512],
                        delta[:, csl_ * 1024 + ts_ * 512:
                              csl_ * 1024 + (ts_ + 1) * 512],
                        spw[:, 0:512], op=ALU.add))(csl, ts),
                        reads=[bspw], writes=[bdelta[csl]])
                P.op("dve", (lambda csl_: lambda e: e.tensor_tensor(
                    du[:, csl_ * 1024:(csl_ + 1) * 1024],
                    delta[:, csl_ * 1024:(csl_ + 1) * 1024],
                    xcr[:, 0:1024], op=ALU.mult))
                    (csl), reads=[bdelta[csl], bxcr], writes=[bdu[csl]])
                for s in range(16):
                    ssl = s % 2
                    P.op("act", (lambda csl_, ssl_, cc_, s_: lambda e:
                         e.activation(
                             dab[:, ssl_ * 1024:(ssl_ + 1) * 1024],
                             delta[:, csl_ * 1024:(csl_ + 1) * 1024], AF.Exp,
                             scale=a_sb[:, cc_ * 16 + s_:cc_ * 16 + s_ + 1]))
                         (csl, ssl, cc, s),
                         reads=[bdelta[csl]], sreads=[wb["a"]],
                         writes=[bda[ssl]])
                    P.op("dve", (lambda csl_, ssl_, s_: lambda e:
                         e.tensor_tensor(
                             dbu[:, ssl_ * 1024:(ssl_ + 1) * 1024],
                             du[:, csl_ * 1024:(csl_ + 1) * 1024],
                             bct[:, s_ * 1024:(s_ + 1) * 1024],
                             op=ALU.mult))(csl, ssl, s),
                         reads=[bdu[csl], bbc[s]], writes=[bdbu[ssl]])
                    if th == 0:
                        P.op("dve", (lambda ssl_: lambda e:
                             e.tensor_tensor_scan(
                                 hb[:, ssl_ * 1024:(ssl_ + 1) * 1024],
                                 dab[:, ssl_ * 1024:(ssl_ + 1) * 1024],
                                 dbu[:, ssl_ * 1024:(ssl_ + 1) * 1024],
                                 0.0, op0=ALU.mult, op1=ALU.add))(ssl),
                             reads=[bda[ssl], bdbu[ssl]], writes=[bh[ssl]])
                        P.op("dve", (lambda ssl_, cc_, s_: lambda e:
                             e.tensor_copy(
                                 hlast[:, cc_ * 16 + s_:cc_ * 16 + s_ + 1],
                                 hb[:, ssl_ * 1024 + 1023:
                                    ssl_ * 1024 + 1024]))(ssl, cc, s),
                             reads=[bh[ssl]], writes=[bhl[cc * 16 + s]])
                    else:
                        P.op("dve", (lambda ssl_, cc_, s_: lambda e:
                             e.tensor_tensor_scan(
                                 hb[:, ssl_ * 1024:(ssl_ + 1) * 1024],
                                 dab[:, ssl_ * 1024:(ssl_ + 1) * 1024],
                                 dbu[:, ssl_ * 1024:(ssl_ + 1) * 1024],
                                 hlast[:, cc_ * 16 + s_:cc_ * 16 + s_ + 1],
                                 op0=ALU.mult, op1=ALU.add))(ssl, cc, s),
                             reads=[bda[ssl], bdbu[ssl]],
                             sreads=[bhl[cc * 16 + s]], writes=[bh[ssl]])
                    P.op("pool", (lambda ssl_, s_: lambda e: e.tensor_tensor(
                        qb[:, 0:1024],
                        hb[:, ssl_ * 1024:(ssl_ + 1) * 1024],
                        bct[:, (16 + s_) * 1024:(17 + s_) * 1024],
                        op=ALU.mult))(ssl, s),
                        reads=[bh[ssl], bbc[16 + s]], writes=[bq])

                    def yacc(s_):
                        def f(e):
                            r = None
                            for tq in range(2):
                                r = e.matmul(
                                    bank(2 + tq), id_b[:, :],
                                    qb[:, tq * 512:(tq + 1) * 512],
                                    start=(s_ == 0), stop=(s_ == 15))
                            return r
                        return f
                    P.op("pe", yacc(s), reads=[bq, wb["idb"]],
                         writes=[psb[2], psb[3]])

                def evac_ys(e):
                    r = None
                    for tq in range(2):
                        r = e.copy(yst[:, tq * 512:(tq + 1) * 512],
                                   bank(2 + tq))
                    return r
                P.op("act", evac_ys, reads=[psb[2], psb[3]], writes=[bys])
                P.op("qact", (lambda b_, cc_, th_: lambda e: e.dma_start(
                    ys_h[b_, cc_, :, th_ * 1024:(th_ + 1) * 1024],
                    yst[:, 0:1024]))(b, cc, th),
                    reads=[bys], dma=True)

    pf.reset()
    pb.reset()
    fence = P.fence()
    wouf = pf.take(2 * 1024)
    ysr = pf.take(2 * 512)
    xcr2 = pf.take(2 * 512)
    mt = pf.take(2 * 512)
    woub = pb.take(16 * 8 * 128)
    wzr = pb.take(2 * 512)
    y2 = pb.take(2 * 512)
    bwouf = nb(2, fence)
    bwoub = nb(1, fence)
    bysr = nb(2, fence)
    bxcr2 = nb(2, fence)
    bwzr = nb(2, fence)
    by2 = nb(2, fence)
    bmt = nb(2, fence)
    for kk in range(16):
        sl = kk % 2
        P.op("qsp", (lambda kk_, sl_: lambda e: e.dma_start(
            wouf[:, sl_ * 1024:(sl_ + 1) * 1024].rearrange(
                "p (m f) -> p m f", m=8),
            w_out[kk_].rearrange("m p f -> p m f")))(kk, sl),
            writes=[bwouf[sl]], dma=True)
        P.op("dve", (lambda kk_, sl_: lambda e: e.tensor_copy(
            woub[:, kk_ * 1024:(kk_ + 1) * 1024],
            wouf[:, sl_ * 1024:(sl_ + 1) * 1024]))(kk, sl),
            reads=[bwouf[sl]], writes=[bwoub])
    for b in range(NB):
        for tc in range(4):
            for cc in range(16):
                sl = cc % 2
                P.op("qsp", (lambda b_, cc_, tc_, sl_: lambda e: e.dma_start(
                    ysr[:, sl_ * 512:(sl_ + 1) * 512],
                    ys_h[b_, cc_, :, tc_ * 512:(tc_ + 1) * 512]))
                    (b, cc, tc, sl), writes=[bysr[sl]], dma=True)
                P.op("qsp", (lambda b_, cc_, tc_, sl_: lambda e: e.dma_start(
                    xcr2[:, sl_ * 512:(sl_ + 1) * 512],
                    xc_h[b_, cc_, :, tc_ * 512:(tc_ + 1) * 512]))
                    (b, cc, tc, sl), writes=[bxcr2[sl]], dma=True)
                P.op("qsp", (lambda b_, cc_, tc_, sl_: lambda e: e.dma_start(
                    wzr[:, sl_ * 512:(sl_ + 1) * 512],
                    wz_h[b_, cc_, :, tc_ * 512:(tc_ + 1) * 512]))
                    (b, cc, tc, sl), writes=[bwzr[sl]], dma=True)
                P.op("dve", (lambda sl_, cc_: lambda e:
                     e.scalar_tensor_tensor(
                         ysr[:, sl_ * 512:(sl_ + 1) * 512],
                         xcr2[:, sl_ * 512:(sl_ + 1) * 512],
                         dp_sb[:, cc_:cc_ + 1],
                         ysr[:, sl_ * 512:(sl_ + 1) * 512],
                         op0=ALU.mult, op1=ALU.add))(sl, cc),
                     reads=[bysr[sl], bxcr2[sl]], sreads=[wb["dp"]],
                     writes=[bysr[sl]])
                P.op("dve", (lambda sl_: lambda e: e.tensor_tensor(
                    y2[:, sl_ * 512:(sl_ + 1) * 512],
                    ysr[:, sl_ * 512:(sl_ + 1) * 512],
                    wzr[:, sl_ * 512:(sl_ + 1) * 512], op=ALU.mult))(sl),
                    reads=[bysr[sl], bwzr[sl]], writes=[by2[sl]])
                for dm in range(8):
                    P.op("pe", (lambda cc_, dm_, sl_: lambda e: e.matmul(
                        bank(dm_), woub[:, (cc_ * 8 + dm_) * 128:
                                        (cc_ * 8 + dm_ + 1) * 128],
                        y2[:, sl_ * 512:(sl_ + 1) * 512],
                        start=(cc_ == 0), stop=(cc_ == 15)))(cc, dm, sl),
                        reads=[by2[sl], bwoub], writes=[psb[dm]])
            for dm in range(8):
                msl = dm % 2
                P.op("act", (lambda dm_, msl_: lambda e: e.copy(
                    mt[:, msl_ * 512:(msl_ + 1) * 512], bank(dm_)))(dm, msl),
                    reads=[psb[dm]], writes=[bmt[msl]])
                P.op("qpool", (lambda b_, tc_, dm_, msl_: lambda e:
                     e.dma_start(
                         m_h[dm_, :, b_ * T + tc_ * 512:
                             b_ * T + (tc_ + 1) * 512],
                         mt[:, msl_ * 512:(msl_ + 1) * 512]))
                     (b, tc, dm, msl), reads=[bmt[msl]], dma=True)

    pf.reset()
    pb.reset()
    fence = P.fence()
    srow = pf.take(1536)
    wgf = pf.take(2 * 1024)
    d1 = pf.take(8 * 512)
    gev = pf.take(2 * 512)
    repsg = pf.take(2 * 512)
    wgb = pb.take(2 * 1024)
    d1b = pb.take(8 * 512)
    sqg = pb.take(512)
    xgb = pb.take(512)
    bwgf = nb(2, fence)
    bwgb = nb(2, fence)
    bd1 = nb(8, fence)
    bgev = nb(2, fence)
    brepsg = nb(1, fence)
    bd1b = nb(8, fence)
    bsqg = nb(1, fence)
    bxgb = nb(1, fence)
    browg = nb(1, fence)
    browd_g = nb(1, fence)

    def wload_g(wten, dm, sl):
        def f(e):
            return e.dma_start(
                wgf[:, sl * 1024:(sl + 1) * 1024].rearrange(
                    "p (k f) -> p k f", k=8),
                wten[:, dm].rearrange("k p f -> p k f"))
        return f

    def mmg(sl_, tok0_, pk_):
        def f(e):
            r = None
            for k in range(8):
                r = e.matmul(
                    bank(pk_),
                    wgb[:, sl_ * 1024 + k * 128:sl_ * 1024 + (k + 1) * 128],
                    xnb[:, k * NTOK + tok0_:k * NTOK + tok0_ + 512],
                    start=(k == 0), stop=(k == 7))
            return r
        return f

    for g8 in range(8):
        tok0 = g8 * 512
        for dm in range(8):
            sl = dm % 2
            P.op("qsp", wload_g(w_gate, dm, sl), writes=[bwgf[sl]], dma=True)
            P.op("dve", (lambda sl_: lambda e: e.tensor_copy(
                wgb[:, sl_ * 1024:(sl_ + 1) * 1024],
                wgf[:, sl_ * 1024:(sl_ + 1) * 1024]))(sl),
                reads=[bwgf[sl]], writes=[bwgb[sl]])
            P.op("pe", mmg(sl, tok0, sl), reads=[bwgb[sl], bxnb],
                 writes=[psb[sl]])
            P.op("act", (lambda sl_, dm_: lambda e: e.activation(
                gev[:, sl_ * 512:(sl_ + 1) * 512], bank(sl_), AF.Sigmoid,
                bias=gateb_sb[:, dm_:dm_ + 1]))(sl, dm),
                reads=[psb[sl]], sreads=[wb["gateb"]], writes=[bgev[sl]])
            P.op("qpool", (lambda dm_, tok0_, sl_: lambda e: e.dma_start(
                gt_h[dm_, :, tok0_:tok0_ + 512],
                gev[:, sl_ * 512:(sl_ + 1) * 512]))(dm, tok0, sl),
                reads=[bgev[sl]], dma=True)
        for dm in range(8):
            sl = dm % 2
            P.op("qsp", wload_g(w_fd1, dm, sl), writes=[bwgf[sl]], dma=True)
            P.op("dve", (lambda sl_: lambda e: e.tensor_copy(
                wgb[:, sl_ * 1024:(sl_ + 1) * 1024],
                wgf[:, sl_ * 1024:(sl_ + 1) * 1024]))(sl),
                reads=[bwgf[sl]], writes=[bwgb[sl]])
            P.op("pe", mmg(sl, tok0, sl), reads=[bwgb[sl], bxnb],
                 writes=[psb[sl]])
            P.op("dve", (lambda sl_, dm_: lambda e: e.tensor_scalar(
                d1[:, dm_ * 512:(dm_ + 1) * 512], bank(sl_),
                fdb1_sb[:, dm_:dm_ + 1], None, op0=ALU.add))(sl, dm),
                reads=[psb[sl]], sreads=[wb["fdb1"]], writes=[bd1[dm]])
            P.op("dve", (lambda dm_: lambda e: e.tensor_copy(
                xgb[:, :], d1[:, dm_ * 512:(dm_ + 1) * 512]))(dm),
                reads=[bd1[dm]], writes=[bxgb])
            P.op("act", (lambda dm_: lambda e: e.activation(
                sqg[:, :], d1[:, dm_ * 512:(dm_ + 1) * 512], AF.Square))(dm),
                reads=[bd1[dm]], writes=[bsqg])
            P.op("pe", (lambda dm_: lambda e: e.matmul(
                bank(4, parts=1), ones_sb[:, :], xgb[:, :],
                start=(dm_ == 0), stop=(dm_ == 7)))(dm),
                reads=[bxgb, wb["ones"]], writes=[psb[4]])
            P.op("pe", (lambda dm_: lambda e: e.matmul(
                bank(5, parts=1), ones_sb[:, :], sqg[:, :],
                start=(dm_ == 0), stop=(dm_ == 7)))(dm),
                reads=[bsqg, wb["ones"]], writes=[psb[5]])

        def rowmath_g(e):
            e.tensor_copy(srow[0:1, 0:512], bank(4, parts=1))
            e.tensor_copy(srow[0:1, 512:1024], bank(5, parts=1))
            e.tensor_scalar_mul(srow[0:1, 0:512], srow[0:1, 0:512], 1.0 / D)
            e.tensor_scalar_mul(srow[0:1, 512:1024], srow[0:1, 512:1024],
                                1.0 / D)
            e.drain(fusable=True)
            e.tensor_tensor(srow[0:1, 1024:1536], srow[0:1, 0:512],
                            srow[0:1, 0:512], op=ALU.mult)
            e.drain(fusable=True)
            return e.tensor_tensor(srow[0:1, 512:1024], srow[0:1, 512:1024],
                                   srow[0:1, 1024:1536], op=ALU.subtract)
        P.op("dve", rowmath_g, reads=[psb[4], psb[5]], writes=[browg])

        def rrow_g(e):
            e.activation(srow[0:1, 512:1024], srow[0:1, 512:1024], AF.Ln,
                         bias=kst[0:1, 0:1])
            e.drain(fusable=True)
            return e.activation(srow[0:1, 512:1024], srow[0:1, 512:1024],
                                AF.Exp, scale=kst[0:1, 2:3])
        P.op("act", rrow_g, reads=[browg], sreads=[wb["kst"]],
             writes=[browg])
        P.op("pe", lambda e: e.matmul(
            bank(6), onesr[:, :], srow[0:1, 0:512], start=True, stop=True),
            reads=[browg, wb["onesr"]], writes=[psb[6]])
        P.op("pe", lambda e: e.matmul(
            bank(7), onesr[:, :], srow[0:1, 512:1024], start=True,
            stop=True), reads=[browg, wb["onesr"]], writes=[psb[7]])
        P.op("dve", lambda e: e.tensor_copy(repsg[:, 0:512], bank(6)),
             reads=[psb[6]], writes=[brepsg])
        P.op("dve", lambda e: e.tensor_copy(repsg[:, 512:1024], bank(7)),
             reads=[psb[7]], writes=[brepsg])
        for dm in range(8):
            def normd(dm_):
                def f(e):
                    e.tensor_tensor(d1[:, dm_ * 512:(dm_ + 1) * 512],
                                    d1[:, dm_ * 512:(dm_ + 1) * 512],
                                    repsg[:, 0:512], op=ALU.subtract)
                    e.drain(fusable=True)
                    e.tensor_tensor(d1[:, dm_ * 512:(dm_ + 1) * 512],
                                    d1[:, dm_ * 512:(dm_ + 1) * 512],
                                    repsg[:, 512:1024], op=ALU.mult)
                    e.drain(fusable=True)
                    e.tensor_scalar(d1[:, dm_ * 512:(dm_ + 1) * 512],
                                    d1[:, dm_ * 512:(dm_ + 1) * 512],
                                    fdg_sb[:, dm_:dm_ + 1],
                                    fdbb_sb[:, dm_:dm_ + 1],
                                    op0=ALU.mult, op1=ALU.add)
                    e.drain(fusable=True)
                    e.tensor_relu(d1[:, dm_ * 512:(dm_ + 1) * 512],
                                  d1[:, dm_ * 512:(dm_ + 1) * 512])
                    e.drain(fusable=True)
                    return e.tensor_copy(d1b[:, dm_ * 512:(dm_ + 1) * 512],
                                         d1[:, dm_ * 512:(dm_ + 1) * 512])
                return f
            P.op("dve", normd(dm), reads=[bd1[dm], brepsg],
                 sreads=[wb["fdg"], wb["fdbb"]], writes=[bd1[dm], bd1b[dm]])
        for dm in range(8):
            sl = dm % 2
            P.op("qsp", wload_g(w_fd2, dm, sl), writes=[bwgf[sl]], dma=True)
            P.op("dve", (lambda sl_: lambda e: e.tensor_copy(
                wgb[:, sl_ * 1024:(sl_ + 1) * 1024],
                wgf[:, sl_ * 1024:(sl_ + 1) * 1024]))(sl),
                reads=[bwgf[sl]], writes=[bwgb[sl]])

            def mmd(sl_, pk_):
                def f(e):
                    r = None
                    for k in range(8):
                        r = e.matmul(
                            bank(pk_),
                            wgb[:, sl_ * 1024 + k * 128:
                                sl_ * 1024 + (k + 1) * 128],
                            d1b[:, k * 512:(k + 1) * 512],
                            start=(k == 0), stop=(k == 7))
                    return r
                return f
            P.op("pe", mmd(sl, sl), reads=[bwgb[sl]] + bd1b,
                 writes=[psb[sl]])
            P.op("dve", (lambda sl_, dm_: lambda e: e.tensor_scalar(
                gev[:, sl_ * 512:(sl_ + 1) * 512], bank(sl_),
                fdb2_sb[:, dm_:dm_ + 1], None, op0=ALU.add))(sl, dm),
                reads=[psb[sl]], sreads=[wb["fdb2"]], writes=[bgev[sl]])
            P.op("qpool", (lambda dm_, tok0_, sl_: lambda e: e.dma_start(
                dt_h[dm_, :, tok0_:tok0_ + 512],
                gev[:, sl_ * 512:(sl_ + 1) * 512]))(dm, tok0, sl),
                reads=[bgev[sl]], dma=True)

    pf.reset()
    pb.reset()
    fence = P.fence()
    srow = pf.take(1536)
    ld3 = pf.take(3 * 512)
    gld = pf.take(512)
    outw = pf.take(8 * 512)
    xtm = pf.take(2 * 1024)
    repx = pf.take(2 * 512)
    xw = pf.take(2 * 512)
    prodw = pf.take(512)
    otw = pf.take(1024)
    sqx = pb.take(512)
    xbx = pb.take(512)
    bld = nb(3, fence)
    bgld = nb(2, fence)
    bout = nb(8, fence)
    bxtm = nb(2, fence)
    brepx = nb(1, fence)
    bxw = nb(1, fence)
    bprod = nb(1, fence)
    bot = nb(1, fence)
    bsqx = nb(1, fence)
    bxbx = nb(1, fence)
    browx = nb(1, fence)
    browd_x = nb(1, fence)
    boutd = [Buf(fence) for _ in range(32)]
    for b in range(NB):
        for tc in range(4):
            tok0 = b * T + tc * 512
            for dm in range(8):
                sl = 0
                for j, ten in enumerate([m_h, gt_h, dt_h]):
                    P.op("qsp", (lambda ten_, dm_, tok0_, j_:
                         lambda e: e.dma_start(
                             ld3[:, j_ * 512:(j_ + 1) * 512],
                             ten_[dm_, :, tok0_:tok0_ + 512]))
                         (ten, dm, tok0, j),
                         writes=[bld[j]], dma=True)

                def comb(sl_, dm_, tok0_):
                    mr, gr, dr = 0, 512, 1024

                    def f(e):
                        e.tensor_scalar_mul(xw[:, 0:512], ld3[:, mr:mr + 512],
                                            f_sb[:, 0:1])
                        e.drain(fusable=True)
                        e.scalar_tensor_tensor(
                            xw[:, 0:512], ld3[:, dr:dr + 512], f_sb[:, 1:2],
                            xw[:, 0:512], op0=ALU.mult, op1=ALU.add)
                        e.tensor_scalar(xw[:, 512:1024], ld3[:, gr:gr + 512],
                                        -1.0, 1.0, op0=ALU.mult, op1=ALU.add)
                        e.drain(fusable=True)
                        e.tensor_tensor(xw[:, 0:512], xw[:, 0:512],
                                        xw[:, 512:1024], op=ALU.mult)
                        e.tensor_tensor(
                            xw[:, 512:1024],
                            xnb[:, dm_ * NTOK + tok0_:
                                dm_ * NTOK + tok0_ + 512],
                            ld3[:, gr:gr + 512], op=ALU.mult)
                        e.drain(fusable=True)
                        return e.tensor_tensor(
                            outw[:, dm_ * 512:(dm_ + 1) * 512], xw[:, 0:512],
                            xw[:, 512:1024], op=ALU.add)
                    return f
                P.op("dve", comb(sl, dm, tok0),
                     reads=[bld[0], bld[1], bld[2], bxnb],
                     sreads=[wb["f"]], writes=[bout[dm], bxw])
                P.op("dve", (lambda dm_: lambda e: e.tensor_copy(
                    xbx[:, :], outw[:, dm_ * 512:(dm_ + 1) * 512]))(dm),
                    reads=[bout[dm]], writes=[bxbx])
                P.op("act", (lambda dm_: lambda e: e.activation(
                    sqx[:, :], outw[:, dm_ * 512:(dm_ + 1) * 512],
                    AF.Square))(dm),
                    reads=[bout[dm]], writes=[bsqx])
                P.op("pe", (lambda dm_: lambda e: e.matmul(
                    bank(0, parts=1), ones_sb[:, :], xbx[:, :],
                    start=(dm_ == 0), stop=(dm_ == 7)))(dm),
                    reads=[bxbx, wb["ones"]], writes=[psb[0]])
                P.op("pe", (lambda dm_: lambda e: e.matmul(
                    bank(1, parts=1), ones_sb[:, :], sqx[:, :],
                    start=(dm_ == 0), stop=(dm_ == 7)))(dm),
                    reads=[bsqx, wb["ones"]], writes=[psb[1]])

            def rowmath_x(e):
                e.tensor_copy(srow[0:1, 0:512], bank(0, parts=1))
                e.tensor_copy(srow[0:1, 512:1024], bank(1, parts=1))
                e.tensor_scalar_mul(srow[0:1, 0:512], srow[0:1, 0:512],
                                    1.0 / D)
                e.tensor_scalar_mul(srow[0:1, 512:1024],
                                    srow[0:1, 512:1024], 1.0 / D)
                e.drain(fusable=True)
                e.tensor_tensor(srow[0:1, 1024:1536], srow[0:1, 0:512],
                                srow[0:1, 0:512], op=ALU.mult)
                e.drain(fusable=True)
                return e.tensor_tensor(srow[0:1, 512:1024],
                                       srow[0:1, 512:1024],
                                       srow[0:1, 1024:1536],
                                       op=ALU.subtract)
            P.op("dve", rowmath_x, reads=[psb[0], psb[1]], writes=[browx])

            def rrow_x(e):
                e.activation(srow[0:1, 512:1024], srow[0:1, 512:1024],
                             AF.Ln, bias=kst[0:1, 0:1])
                e.drain(fusable=True)
                return e.activation(srow[0:1, 512:1024],
                                    srow[0:1, 512:1024], AF.Exp,
                                    scale=kst[0:1, 2:3])
            P.op("act", rrow_x, reads=[browx], sreads=[wb["kst"]],
                 writes=[browx])
            P.op("pe", lambda e: e.matmul(
                bank(6), onesr[:, :], srow[0:1, 0:512], start=True,
                stop=True), reads=[browx, wb["onesr"]], writes=[psb[6]])
            P.op("pe", lambda e: e.matmul(
                bank(7), onesr[:, :], srow[0:1, 512:1024], start=True,
                stop=True), reads=[browx, wb["onesr"]], writes=[psb[7]])
            P.op("dve", lambda e: e.tensor_copy(repx[:, 0:512], bank(6)),
                 reads=[psb[6]], writes=[brepx])
            P.op("dve", lambda e: e.tensor_copy(repx[:, 512:1024], bank(7)),
                 reads=[psb[7]], writes=[brepx])
            for dm in range(8):
                gsl = 0
                P.op("qsp", (lambda dm_, tok0_: lambda e: e.dma_start(
                    gld[:, 0:512],
                    g_h[dm_, :, tok0_:tok0_ + 512]))(dm, tok0),
                    writes=[bgld[0]], dma=True)

                def sdm(dm_):
                    def f(e):
                        e.tensor_tensor(outw[:, dm_ * 512:(dm_ + 1) * 512],
                                        outw[:, dm_ * 512:(dm_ + 1) * 512],
                                        repx[:, 0:512], op=ALU.subtract)
                        e.drain(fusable=True)
                        e.tensor_tensor(outw[:, dm_ * 512:(dm_ + 1) * 512],
                                        outw[:, dm_ * 512:(dm_ + 1) * 512],
                                        repx[:, 512:1024], op=ALU.mult)
                        e.drain(fusable=True)
                        return e.tensor_scalar(
                            outw[:, dm_ * 512:(dm_ + 1) * 512],
                            outw[:, dm_ * 512:(dm_ + 1) * 512],
                            nrmg_sb[:, dm_:dm_ + 1], nrmb_sb[:, dm_:dm_ + 1],
                            op0=ALU.mult, op1=ALU.add)
                    return f
                P.op("dve", sdm(dm), reads=[bout[dm], brepx],
                     sreads=[wb["nrmg"], wb["nrmb"]], writes=[bout[dm]])
                P.op("dve", (lambda dm_: lambda e: e.tensor_tensor(
                    prodw[:, 0:512], outw[:, dm_ * 512:(dm_ + 1) * 512],
                    gld[:, 0:512], op=ALU.mult))(dm),
                    reads=[bout[dm], bgld[0]], writes=[bprod])

                def trx(dm_):
                    def f(e):
                        r = None
                        for tt in range(4):
                            bk = tt * 2 + (dm_ // 4)
                            r = e.transpose(
                                bank(bk)[:, (dm_ % 4) * 128:
                                         (dm_ % 4 + 1) * 128],
                                prodw[:, tt * 128:(tt + 1) * 128],
                                id_f[:, :])
                        return r
                    return f
                P.op("pe", trx(dm), reads=[bprod, wb["idf"]],
                     writes=[psb[tt * 2 + (dm // 4)] for tt in range(4)])
            for tt in range(4):
                xsl = tt % 2
                P.op("qsp", (lambda b_, tc_, tt_, xsl_: lambda e:
                     e.dma_start(
                         xtm[:, xsl_ * 1024:(xsl_ + 1) * 1024],
                         x_in[b_, tc_ * 512 + tt_ * 128:
                              tc_ * 512 + (tt_ + 1) * 128, :]))
                     (b, tc, tt, xsl), writes=[bxtm[xsl]], dma=True)

                def addx(tt_, xsl_):
                    def f(e):
                        e.tensor_tensor(
                            otw[:, 0:512], ps_t[:, tt_ * 1024:
                                                tt_ * 1024 + 512],
                            xtm[:, xsl_ * 1024:xsl_ * 1024 + 512],
                            op=ALU.add)
                        return e.tensor_tensor(
                            otw[:, 512:1024],
                            ps_t[:, tt_ * 1024 + 512:(tt_ + 1) * 1024],
                            xtm[:, xsl_ * 1024 + 512:(xsl_ + 1) * 1024],
                            op=ALU.add)
                    return f
                P.op("dve", addx(tt, xsl),
                     reads=[bxtm[xsl], psb[tt * 2], psb[tt * 2 + 1]],
                     writes=[bot])
                oi = (b * 4 + tc) * 4 + tt
                P.op("qsp", (lambda b_, tc_, tt_: lambda e: e.dma_start(
                    out_t[b_, tc_ * 512 + tt_ * 128:
                          tc_ * 512 + (tt_ + 1) * 128, :], otw[:, 0:1024]))
                    (b, tc, tt), reads=[bot], writes=[boutd[oi]], dma=True)

    if DBG:
        for off_, ten_ in [(0, ffng_sb), (8, ffnb_sb), (16, nrmg_sb),
                           (24, nrmb_sb), (32, gateb_sb), (40, kst)]:
            P.op("qsp", (lambda o_, t_: lambda e: e.dma_start(
                prm_d[:, o_:o_ + 8], t_[:, :]))(off_, ten_),
                reads=boutd, dma=True)
    P.op("qsp", lambda e: e.nop(), reads=boutd)

    import os
    if os.environ.get("BASS_NOPS"):
        print("total ops:", len(P.ops))
    P.emit(nc, ctx)
    ctx.close()
    return nc


_NC_CACHE = {}


def _get_nc():
    if "nc" not in _NC_CACHE:
        _NC_CACHE["nc"] = build()
    return _NC_CACHE["nc"]


def _prep_weights(i):
    c = np.ascontiguousarray
    w = {}
    w["w_in"] = c(i["L0_attn_in_proj_w"].T.reshape(8, 128, 32, 128)
                  .transpose(0, 2, 1, 3))
    w["w_xp"] = c(i["L0_attn_x_proj_w"].T.reshape(16, 128, 96))
    w["w_dt"] = c(i["L0_attn_dt_proj_w"].T.reshape(64, 16, 128)
                  .transpose(1, 0, 2))
    w["dt_b"] = c(i["L0_attn_dt_proj_b"].reshape(16, 128).T)
    w["a_log"] = c(i["L0_attn_A_log"].reshape(16, 128, 16))
    w["dp"] = c(i["L0_attn_Dp"].reshape(16, 128).T)
    w["convw"] = c(i["L0_attn_conv_w"].reshape(16, 128, 4))
    w["convb"] = c(i["L0_attn_conv_b"].reshape(16, 128).T)
    w["w_out"] = c(i["L0_attn_out_proj_w"].T.reshape(16, 128, 8, 128)
                   .transpose(0, 2, 1, 3))
    w["w_gate"] = c(i["L0_attn_gate_w"].T.reshape(8, 128, 8, 128)
                    .transpose(0, 2, 1, 3))
    w["gate_b"] = c(i["L0_attn_gate_b"].reshape(8, 128).T)
    w["w_fd1"] = c(i["L0_attn_fd_w1"].T.reshape(8, 128, 8, 128)
                   .transpose(0, 2, 1, 3))
    w["fd_b1"] = c(i["L0_attn_fd_b1"].reshape(8, 128).T)
    w["fd_g"] = c(i["L0_attn_fd_ln_g"].reshape(8, 128).T)
    w["fd_bb"] = c(i["L0_attn_fd_ln_b"].reshape(8, 128).T)
    w["w_fd2"] = c(i["L0_attn_fd_w2"].T.reshape(8, 128, 8, 128)
                   .transpose(0, 2, 1, 3))
    w["fd_b2"] = c(i["L0_attn_fd_b2"].reshape(8, 128).T)
    w["nrm_g"] = c(i["L0_attn_norm_g"].reshape(8, 128).T)
    w["nrm_b"] = c(i["L0_attn_norm_b"].reshape(8, 128).T)
    w["bal"] = c(i["L0_attn_balance_factor"].reshape(1, 1))
    w["w_r"] = c(i["L0_ffn_W_router"].reshape(8, 128, 4))
    w["w1"] = c(i["L0_ffn_W1"].reshape(4, 8, 128, 32, 128)
                .transpose(3, 0, 1, 2, 4))
    w["b1"] = c(i["L0_ffn_b1"].reshape(4 * 32, 128).T)
    w["wo"] = c(i["L0_ffn_Wo"].reshape(32, 128, 8, 128).transpose(0, 2, 1, 3))
    w["bo"] = c(i["L0_ffn_bo"].reshape(8, 128).T)
    w["ffn_g"] = c(i["L0_ffn_ln_gamma"].reshape(8, 128).T)
    w["ffn_b"] = c(i["L0_ffn_ln_beta"].reshape(8, 128).T)
    kst = np.zeros((128, 8), dtype=np.float32)
    kst[:, 0] = EPS1
    kst[:, 1] = EPSF
    kst[:, 2] = -0.5
    kst[:, 3] = -1.0
    kst[:, 4] = 1.0
    w["kst"] = kst
    w["onesr_i"] = np.ones((1, 128), dtype=np.float32)
    w["iden"] = np.eye(128, dtype=np.float32)
    import ml_dtypes
    w["idenb"] = np.eye(128, dtype=ml_dtypes.bfloat16)
    w["onesb"] = np.ones((128, 1), dtype=ml_dtypes.bfloat16)
    return w


def kernel(**inputs):
    nc = _get_nc()
    w = _prep_weights(inputs)
    x = np.ascontiguousarray(inputs["L0_attn_x"], dtype=np.float32)
    in_maps = []
    for core in range(8):
        m = dict(w)
        m["x"] = np.ascontiguousarray(x[core * NB:(core + 1) * NB])
        in_maps.append(m)
    res = run_bass_kernel_spmd(nc, in_maps, core_ids=list(range(8)))
    out = np.concatenate([np.asarray(res.results[c]["out"])
                          for c in range(8)], axis=0)
    return out.reshape(16, T, D).astype(np.float32, copy=False)
